# version 01
# baseline (speedup 1.0000x reference)
import numpy as np
import ml_dtypes
from contextlib import ExitStack
import concourse.bass as bass
import concourse.mybir as mybir
from concourse.bass_utils import run_bass_kernel_spmd
from concourse.alu_op_type import AluOpType

F32 = mybir.dt.float32
BF16 = mybir.dt.bfloat16
AF = mybir.ActivationFunctionType
AXX = mybir.AxisListType.X
OP = AluOpType

NCORES = 8
D = 1024
T = 16384
TC = T // NCORES
NSEQ = 4096
B = 4
NH = 16
HD = 64
LN_EPS = 1e-5
NS = TC // 128

QUEUES = ("pe", "act", "dve", "pool", "sp")


class Sched:

    def __init__(self):
        self.ops = {q: [] for q in QUEUES}
        self.cnt = {q: 0 for q in QUEUES}
        self.last_w = {}
        self.readers = {}
        self.watermark = {q: {q2: 0 for q2 in QUEUES} for q in QUEUES}
        self.pending_fence = {q: [] for q in QUEUES}

    def emit(self, q, fn, reads=(), writes=(), inc=1, extra_deps=()):
        deps = list(extra_deps)
        if self.pending_fence[q]:
            deps.extend(self.pending_fence[q])
            self.pending_fence[q] = []
        for b_ in reads:
            t = self.last_w.get(b_)
            if t is not None:
                deps.append(t)
        for b_ in writes:
            t = self.last_w.get(b_)
            if t is not None:
                deps.append(t)
            deps.extend(self.readers.get(b_, ()))
        byq = {}
        for (q2, v2) in deps:
            if q2 == "pe" and q == "pe":
                continue
            if v2 > byq.get(q2, -1):
                byq[q2] = v2
        wm = self.watermark[q]
        final = []
        for q2, v2 in sorted(byq.items()):
            if v2 > wm[q2]:
                final.append((q2, v2))
                wm[q2] = v2
        val = self.cnt[q] + inc
        tok = (q, val)
        self.ops[q].append((fn, final, inc))
        self.cnt[q] = val
        for b_ in reads:
            self.readers.setdefault(b_, []).append(tok)
        for b_ in writes:
            self.last_w[b_] = tok
            self.readers[b_] = []
        return tok

    def fence(self):
        toks = [(q, self.cnt[q]) for q in QUEUES if self.cnt[q] > 0]
        for q in QUEUES:
            self.pending_fence[q] = [t for t in toks if t[0] != q]

    def finish(self):
        deps = [(q, self.cnt[q]) for q in QUEUES if q != "sp" and self.cnt[q] > 0]
        self.emit("sp", lambda e: e.nop(), extra_deps=deps)

    def run(self, nc):
        with (
            nc.semaphore("s_pe") as s_pe,
            nc.semaphore("s_act") as s_act,
            nc.semaphore("s_dve") as s_dve,
            nc.semaphore("s_pool") as s_pool,
            nc.semaphore("s_sp") as s_sp,
            nc.Block() as block,
        ):
            sems = {"pe": s_pe, "act": s_act, "dve": s_dve,
                    "pool": s_pool, "sp": s_sp}

            def replay(q, eng):
                for fn, deps, inc in self.ops[q]:
                    for (q2, v2) in deps:
                        eng.wait_ge(sems[q2], v2)
                    fn(eng).then_inc(sems[q], inc)

            @block.tensor
            def _(eng):
                replay("pe", eng)

            @block.scalar
            def _(eng):
                replay("act", eng)

            @block.vector
            def _(eng):
                replay("dve", eng)

            @block.gpsimd
            def _(eng):
                replay("pool", eng)

            @block.sync
            def _(eng):
                replay("sp", eng)


def _f32(x):
    return np.ascontiguousarray(np.asarray(x), dtype=np.float32)


def emit_ln(s, bufs, xr, xr_key, out_ap, out_key, n=D):
    red, cx, sq = bufs
    s.emit("dve", lambda e: e.reduce_sum(red[:, 0:1], xr[:], axis=AXX),
           reads=(xr_key,), writes=("ln_r0",))
    s.emit("dve", lambda e: e.tensor_scalar(red[:, 1:2], red[:, 0:1], -1.0 / n, None, OP.mult),
           reads=("ln_r0",), writes=("ln_r1",))
    s.emit("dve", lambda e: e.tensor_scalar(cx[:], xr[:], red[:, 1:2], None, OP.add),
           reads=(xr_key, "ln_r1"), writes=("ln_cx",))
    s.emit("dve", lambda e: e.tensor_tensor(sq[:], cx[:], cx[:], OP.mult),
           reads=("ln_cx",), writes=("ln_sq",))
    s.emit("dve", lambda e: e.reduce_sum(red[:, 2:3], sq[:], axis=AXX),
           reads=("ln_sq",), writes=("ln_r2",))
    s.emit("dve", lambda e: e.tensor_scalar(red[:, 2:3], red[:, 2:3], 1.0 / n, LN_EPS,
                                            OP.mult, OP.add),
           reads=("ln_r2",), writes=("ln_r2",))
    s.emit("act", lambda e: e.activation(out=red[:, 3:4], in_=red[:, 2:3], func=AF.Sqrt),
           reads=("ln_r2",), writes=("ln_r3",))
    s.emit("dve", lambda e: e.reciprocal(red[:, 2:3], red[:, 3:4]),
           reads=("ln_r3",), writes=("ln_r4",))
    s.emit("dve", lambda e: e.tensor_scalar(out_ap, cx[:], red[:, 2:3], None, OP.mult),
           reads=("ln_cx", "ln_r4"), writes=(out_key,))



def build_launch1():
    nc = bass.Bass("TRN2", target_bir_lowering=False)
    x = nc.dram_tensor("x", [T, D], F32, kind="ExternalInput").ap()
    qwT = nc.dram_tensor("qwT", [D, 128], F32, kind="ExternalInput").ap()
    kwT = nc.dram_tensor("kwT", [D, 128], F32, kind="ExternalInput").ap()
    vwT = nc.dram_tensor("vwT", [D, 128], F32, kind="ExternalInput").ap()
    qb_row = nc.dram_tensor("qb_row", [1, 128], F32, kind="ExternalInput").ap()
    kb_row = nc.dram_tensor("kb_row", [1, 128], F32, kind="ExternalInput").ap()
    vb_row = nc.dram_tensor("vb_row", [1, 128], F32, kind="ExternalInput").ap()
    cosd = nc.dram_tensor("cosd", [NSEQ, 128], BF16, kind="ExternalInput").ap()
    sind = nc.dram_tensor("sind", [NSEQ, 128], BF16, kind="ExternalInput").ap()
    eye_bf = nc.dram_tensor("eye_bf", [128, 128], BF16, kind="ExternalInput").ap()
    eye_f32 = nc.dram_tensor("eye_f32", [128, 128], F32, kind="ExternalInput").ap()
    ones_bf = nc.dram_tensor("ones_bf", [1, 128], BF16, kind="ExternalInput").ap()
    p_sh = nc.dram_tensor("p_sh", [2, B, NSEQ, HD], BF16, kind="ExternalOutput").ap()

    s = Sched()
    with ExitStack() as _es:
        qwT_bf = _es.enter_context(nc.sbuf_tensor("qwT_bf", [128, 1024], BF16))
        kwT_bf = _es.enter_context(nc.sbuf_tensor("kwT_bf", [128, 1024], BF16))
        vwT_bf = _es.enter_context(nc.sbuf_tensor("vwT_bf", [128, 1024], BF16))
        wstg = _es.enter_context(nc.sbuf_tensor("wstg", [128, 1024], F32))
        qbr_s = _es.enter_context(nc.sbuf_tensor("qbr_s", [1, 128], F32))
        kbr_s = _es.enter_context(nc.sbuf_tensor("kbr_s", [1, 128], F32))
        qbr_bf = _es.enter_context(nc.sbuf_tensor("qbr_bf", [1, 128], BF16))
        kbr_bf = _es.enter_context(nc.sbuf_tensor("kbr_bf", [1, 128], BF16))
        vbr_s = _es.enter_context(nc.sbuf_tensor("vbr_s", [1, 128], F32))
        vbr_bf = _es.enter_context(nc.sbuf_tensor("vbr_bf", [1, 128], BF16))
        cos_s = _es.enter_context(nc.sbuf_tensor("cos_s", [128, NSEQ], BF16))
        sin_s = _es.enter_context(nc.sbuf_tensor("sin_s", [128, NSEQ], BF16))
        eye_bf_s = _es.enter_context(nc.sbuf_tensor("eye_bf_s", [128, 128], BF16))
        eye_f32_s = _es.enter_context(nc.sbuf_tensor("eye_f32_s", [128, 128], F32))
        ones_bf_s = _es.enter_context(nc.sbuf_tensor("ones_bf_s", [1, 128], BF16))
        qT = _es.enter_context(nc.sbuf_tensor("qT", [128, T], BF16))
        kT = _es.enter_context(nc.sbuf_tensor("kT", [128, T], BF16))
        v0 = _es.enter_context(nc.sbuf_tensor("v0", [128, 128 * 65], BF16))
        v1 = _es.enter_context(nc.sbuf_tensor("v1", [128, 128 * 65], BF16))
        for name, src, dst in (("qw", qwT, qwT_bf), ("kw", kwT, kwT_bf), ("vw", vwT, vwT_bf)):
            for j in range(8):
                s.emit("sp", lambda e, src=src, j=j: e.dma_start(
                    out=wstg[:, j * 128:(j + 1) * 128], in_=src[j * 128:(j + 1) * 128, :]),
                    writes=("wstg",), inc=16)
            s.emit("act", lambda e, dst=dst: e.activation(out=dst[:], in_=wstg[:], func=AF.Copy),
                   reads=("wstg",), writes=(name,))
        s.emit("sp", lambda e: e.dma_start(out=qbr_s[:], in_=qb_row[:]), writes=("qbk",), inc=16)
        s.emit("sp", lambda e: e.dma_start(out=kbr_s[:], in_=kb_row[:]), writes=("kbk",), inc=16)
        s.emit("sp", lambda e: e.dma_start(out=vbr_s[:], in_=vb_row[:]), writes=("vbr",), inc=16)
        s.emit("act", lambda e: e.activation(out=qbr_bf[:], in_=qbr_s[:], func=AF.Copy),
               reads=("qbk",), writes=("qbb",))
        s.emit("act", lambda e: e.activation(out=kbr_bf[:], in_=kbr_s[:], func=AF.Copy),
               reads=("kbk",), writes=("kbb",))
        s.emit("act", lambda e: e.activation(out=vbr_bf[:], in_=vbr_s[:], func=AF.Copy),
               reads=("vbr",), writes=("vbrb",))
        for p in range(32):
            s.emit("sp", lambda e, p=p: e.dma_start(
                out=cos_s[:, p * 128:(p + 1) * 128], in_=cosd[p * 128:(p + 1) * 128, :]),
                writes=("cos",), inc=16)
            s.emit("sp", lambda e, p=p: e.dma_start(
                out=sin_s[:, p * 128:(p + 1) * 128], in_=sind[p * 128:(p + 1) * 128, :]),
                writes=("sin",), inc=16)
        s.emit("sp", lambda e: e.dma_start(out=eye_bf_s[:], in_=eye_bf[:]), writes=("eyeb",), inc=16)
        s.emit("sp", lambda e: e.dma_start(out=eye_f32_s[:], in_=eye_f32[:]), writes=("eyef",), inc=16)
        s.emit("sp", lambda e: e.dma_start(out=ones_bf_s[:], in_=ones_bf[:]), writes=("ones",), inc=16)
        s.emit("dve", lambda e: e.memset(v0[:], 1.0), writes=("v0i",))
        s.emit("dve", lambda e: e.memset(v1[:], 1.0), writes=("v1i",))

        with ExitStack() as _es:
            xt = _es.enter_context(nc.sbuf_tensor("xt", [128, 4096], F32))
            xbf = _es.enter_context(nc.sbuf_tensor("xbf", [128, 4096], BF16))
            xT0 = _es.enter_context(nc.sbuf_tensor("xT0", [128, 4096], BF16))
            xT1 = _es.enter_context(nc.sbuf_tensor("xT1", [128, 4096], BF16))
            qn_bf = _es.enter_context(nc.sbuf_tensor("qn_bf", [128, 128], BF16))
            qr_bf = _es.enter_context(nc.sbuf_tensor("qr_bf", [128, 128], BF16))
            s1f = _es.enter_context(nc.sbuf_tensor("s1f", [128, 128], F32))
            s2f = _es.enter_context(nc.sbuf_tensor("s2f", [128, 128], F32))
            psT0 = _es.enter_context(nc.psum_tensor("psT0", [128, 128], BF16))
            psT1 = _es.enter_context(nc.psum_tensor("psT1", [128, 128], BF16))
            psQ = _es.enter_context(nc.psum_tensor("psQ", [128, 128], F32))
            psK = _es.enter_context(nc.psum_tensor("psK", [128, 128], F32))
            psV = _es.enter_context(nc.psum_tensor("psV", [128, 128], F32))
            for g in range(32):
                gg = g % 8
                xT = (xT0, xT1)[g % 2]
                kxT = f"xT{g % 2}"
                for st in range(4):
                    s.emit("sp", lambda e, st=st, g=g: e.dma_start(
                        out=xt[:, st * 1024:(st + 1) * 1024],
                        in_=x[g * 512 + st * 128: g * 512 + (st + 1) * 128, :]),
                        writes=("xt",), inc=16)
                s.emit("act", lambda e: e.activation(out=xbf[:], in_=xt[:], func=AF.Copy),
                       reads=("xt",), writes=("xbf",))
                for ic in range(8):
                    for st in range(4):
                        psT = (psT0, psT1)[st % 2]
                        kps = f"psT{st % 2}"
                        s.emit("pe", lambda e, psT=psT, st=st, ic=ic: e.transpose(
                            psT[:], xbf[:, st * 1024 + ic * 128: st * 1024 + (ic + 1) * 128],
                            eye_bf_s[:]),
                            reads=("xbf", "eyeb"), writes=(kps,))
                        s.emit("dve", lambda e, psT=psT, xT=xT, ic=ic, st=st: e.tensor_copy(
                            xT[:, ic * 512 + st * 128: ic * 512 + (st + 1) * 128], psT[:]),
                            reads=(kps,), writes=(kxT,))
                for st in range(4):
                    p = gg * 4 + st
                    ct3 = cos_s[:, p * 128:(p + 1) * 128]
                    sn3 = sin_s[:, p * 128:(p + 1) * 128].rearrange(
                        "p (h f d) -> p h f d", h=2, f=2, d=32)
                    for which, psX, wT_bf, wkey, bbf, bbk, dstT, dkey in (
                            ("q", psQ, qwT_bf, "qw", qbr_bf, "qbb", qT, "qTk"),
                            ("k", psK, kwT_bf, "kw", kbr_bf, "kbb", kT, "kTk")):
                        for ic in range(8):
                            s.emit("pe", lambda e, psX=psX, wT_bf=wT_bf, xT=xT, ic=ic,
                                   st=st: e.matmul(
                                psX[:], xT[:, ic * 512 + st * 128: ic * 512 + (st + 1) * 128],
                                wT_bf[:, ic * 128:(ic + 1) * 128],
                                start=(ic == 0), stop=False, skip_group_check=True),
                                reads=(kxT, wkey), writes=(f"ps{which}",))
                        s.emit("pe", lambda e, psX=psX, bbf=bbf: e.matmul(
                            psX[:], ones_bf_s[:], bbf[:],
                            start=False, stop=True, skip_group_check=True),
                            reads=("ones", bbk), writes=(f"ps{which}",))
                        s.emit("act", lambda e, psX=psX: e.activation(
                            out=qn_bf[:], in_=psX[:], func=AF.Copy),
                            reads=(f"ps{which}",), writes=("qnb",))
                        qn3 = qn_bf.rearrange("p (h f d) -> p h f d", h=2, f=2, d=32)
                        s23 = s2f.rearrange("p (h f d) -> p h f d", h=2, f=2, d=32)
                        s.emit("dve", lambda e, ct3=ct3: e.tensor_tensor(
                            s1f[:], qn_bf[:], ct3, OP.mult),
                            reads=("qnb", "cos"), writes=("s1k",))
                        s.emit("dve", lambda e, qn3=qn3, s23=s23, sn3=sn3: e.tensor_tensor(
                            s23[:, :, 0, :], qn3[:, :, 1, :], sn3[:, :, 0, :], OP.mult),
                            reads=("qnb", "sin"), writes=("s2k",))
                        s.emit("dve", lambda e, qn3=qn3, s23=s23, sn3=sn3: e.tensor_tensor(
                            s23[:, :, 1, :], qn3[:, :, 0, :], sn3[:, :, 1, :], OP.mult),
                            reads=("qnb", "sin"), writes=("s2k",))
                        s.emit("dve", lambda e: e.tensor_tensor(
                            qr_bf[:], s1f[:], s2f[:], OP.add),
                            reads=("s1k", "s2k"), writes=("qrb",))
                        s.emit("pe", lambda e: e.transpose(psT0[:], qr_bf[:], eye_bf_s[:]),
                               reads=("qrb", "eyeb"), writes=("psT0",))
                        s.emit("dve", lambda e, dstT=dstT, g=g, st=st: e.tensor_copy(
                            dstT[:, g * 512 + st * 128: g * 512 + (st + 1) * 128], psT0[:]),
                            reads=("psT0",), writes=(dkey,))
                for st in range(4):
                    jtile = g * 4 + st
                    for ic in range(8):
                        s.emit("pe", lambda e, xT=xT, ic=ic, st=st: e.matmul(
                            psV[:], xT[:, ic * 512 + st * 128: ic * 512 + (st + 1) * 128],
                            vwT_bf[:, ic * 128:(ic + 1) * 128],
                            start=(ic == 0), stop=False, skip_group_check=True),
                            reads=(kxT, "vw"), writes=("psV",))
                    s.emit("pe", lambda e: e.matmul(
                        psV[:], ones_bf_s[:], vbr_bf[:],
                        start=False, stop=True, skip_group_check=True),
                        reads=("ones", "vbrb"), writes=("psV",))
                    s.emit("dve", lambda e, jtile=jtile: e.tensor_copy(
                        v0[:, jtile * 65: jtile * 65 + 64], psV[:, 0:64]),
                        reads=("psV", "v0i"), writes=(f"v0_{jtile}",))
                    s.emit("dve", lambda e, jtile=jtile: e.tensor_copy(
                        v1[:, jtile * 65: jtile * 65 + 64], psV[:, 64:128]),
                        reads=("psV", "v1i"), writes=(f"v1_{jtile}",))
        s.fence()

        with ExitStack() as _es:
            esp = _es.enter_context(nc.sbuf_tensor("esp", [128, 1024], BF16))
            av_s = _es.enter_context(nc.sbuf_tensor("av_s", [65, 512], F32))
            avT_s = _es.enter_context(nc.sbuf_tensor("avT_s", [128, 65], F32))
            avn0 = _es.enter_context(nc.sbuf_tensor("avn0", [128, 64], BF16))
            avn1 = _es.enter_context(nc.sbuf_tensor("avn1", [128, 64], BF16))
            psS0 = _es.enter_context(nc.psum_tensor("psS0", [128, 512], F32))
            psS1 = _es.enter_context(nc.psum_tensor("psS1", [128, 512], F32))
            psAV = _es.enter_context(nc.psum_tensor("psAV", [65, 512], F32))
            psAVT = _es.enter_context(nc.psum_tensor("psAVT", [128, 65], F32))
            for hh in range(2):
                vbuf = (v0, v1)[hh]
                for bb in range(B):
                    for qg in range(8):
                        qc = bb * 4096 + qg * 512
                        r0 = hh * 64

                        def mm_s(e, kc, ps, r0=r0, bb=bb, qc=qc):
                            return e.matmul(
                                ps[:],
                                kT[r0:r0 + 64, bb * 4096 + kc * 128: bb * 4096 + (kc + 1) * 128],
                                qT[r0:r0 + 64, qc:qc + 512],
                                start=True, stop=True, skip_group_check=True)

                        s.emit("pe", lambda e, f=mm_s: f(e, 0, psS0),
                               reads=("qTk", "kTk"), writes=("psS0",))
                        for kc in range(32):
                            ps = (psS0, psS1)[kc % 2]
                            s.emit("act", lambda e, kc=kc, ps=ps: e.activation(
                                out=esp[:, (kc % 2) * 512:(kc % 2 + 1) * 512], in_=ps[:],
                                func=AF.Exp, scale=0.125),
                                reads=(f"psS{kc % 2}",), writes=(f"esp{kc % 2}",))
                            if kc < 31:
                                nps = (psS0, psS1)[(kc + 1) % 2]
                                s.emit("pe", lambda e, f=mm_s, kc=kc, nps=nps: f(e, kc + 1, nps),
                                       reads=("qTk", "kTk"), writes=(f"psS{(kc + 1) % 2}",))
                            jv = bb * 32 + kc
                            s.emit("pe", lambda e, kc=kc, jv=jv, vbuf=vbuf: e.matmul(
                                psAV[:], vbuf[:, jv * 65: jv * 65 + 65],
                                esp[:, (kc % 2) * 512:(kc % 2 + 1) * 512],
                                start=(kc == 0), stop=(kc == 31), skip_group_check=True),
                                reads=(f"esp{kc % 2}", f"v{hh}_{jv}"), writes=("psAV",))
                        s.emit("dve", lambda e: e.tensor_copy(av_s[:], psAV[:]),
                               reads=("psAV",), writes=("av_s",))
                        for j in range(4):
                            avn = (avn0, avn1)[j % 2]
                            kavn = f"avn{j % 2}"
                            s.emit("pe", lambda e, j=j: e.transpose(
                                psAVT[:], av_s[:, j * 128:(j + 1) * 128], eye_f32_s[0:65, 0:65]),
                                reads=("av_s", "eyef"), writes=("psAVT",))
                            s.emit("dve", lambda e: e.tensor_copy(avT_s[:], psAVT[:]),
                                   reads=("psAVT",), writes=("avT_s",))
                            s.emit("dve", lambda e: e.reciprocal(avT_s[:, 64:65], avT_s[:, 64:65]),
                                   reads=("avT_s",), writes=("avT_s",))
                            s.emit("dve", lambda e, avn=avn: e.tensor_scalar(
                                avn[:], avT_s[:, 0:64], avT_s[:, 64:65], None, OP.mult),
                                reads=("avT_s",), writes=(kavn,))
                            s.emit("pool", lambda e, hh=hh, bb=bb, qg=qg, j=j, avn=avn: e.dma_start(
                                out=p_sh[hh, bb, qg * 512 + j * 128: qg * 512 + (j + 1) * 128, :],
                                in_=avn[:]),
                                reads=(kavn,), inc=16)
        s.finish()
        s.run(nc)
    return nc



def build_launch2():
    nc = bass.Bass("TRN2", target_bir_lowering=False)
    p_in = nc.dram_tensor("p_sh", [TC, D], BF16, kind="ExternalInput").ap()
    x_own = nc.dram_tensor("x_own", [TC, D], F32, kind="ExternalInput").ap()
    projT = nc.dram_tensor("projT", [D, D], F32, kind="ExternalInput").ap()
    projb = nc.dram_tensor("projb", [1, D], F32, kind="ExternalInput").ap()
    gatewT = nc.dram_tensor("gatewT", [D, 32], F32, kind="ExternalInput").ap()
    gateb = nc.dram_tensor("gateb", [1, 32], F32, kind="ExternalInput").ap()
    wgT = nc.dram_tensor("wgT", [32, D, 512], F32, kind="ExternalInput").ap()
    wuT = nc.dram_tensor("wuT", [32, D, 512], F32, kind="ExternalInput").ap()
    wdT = nc.dram_tensor("wdT", [32, 512, D], F32, kind="ExternalInput").ap()
    shgT = nc.dram_tensor("shgT", [D, D], F32, kind="ExternalInput").ap()
    shuT = nc.dram_tensor("shuT", [D, D], F32, kind="ExternalInput").ap()
    shdT = nc.dram_tensor("shdT", [D, D], F32, kind="ExternalInput").ap()
    eye_bf = nc.dram_tensor("eye_bf", [128, 128], BF16, kind="ExternalInput").ap()
    eye_f32 = nc.dram_tensor("eye_f32", [128, 128], F32, kind="ExternalInput").ap()
    ones_bf = nc.dram_tensor("ones_bf", [1, 128], BF16, kind="ExternalInput").ap()
    ones_f32 = nc.dram_tensor("ones_f32", [1, 128], F32, kind="ExternalInput").ap()
    x2_sh = nc.dram_tensor("x2_sh", [TC, D], F32, kind="ExternalOutput").ap()

    s = Sched()
    with ExitStack() as _es:
        eye_bf_s = _es.enter_context(nc.sbuf_tensor("eye_bf_s", [128, 128], BF16))
        eye_f32_s = _es.enter_context(nc.sbuf_tensor("eye_f32_s", [128, 128], F32))
        ones_bf_s = _es.enter_context(nc.sbuf_tensor("ones_bf_s", [1, 128], BF16))
        ones_f32_s = _es.enter_context(nc.sbuf_tensor("ones_f32_s", [1, 128], F32))
        x1T = _es.enter_context(nc.sbuf_tensor("x1T", [128, 8 * TC], BF16))
        accum = _es.enter_context(nc.sbuf_tensor("accum", [128, NS * D], F32))
        cmb = _es.enter_context(nc.sbuf_tensor("cmb", [128, NS * 34], F32))
        lred = _es.enter_context(nc.sbuf_tensor("lred", [128, 4], F32))
        lcx = _es.enter_context(nc.sbuf_tensor("lcx", [128, D], F32))
        lsq = _es.enter_context(nc.sbuf_tensor("lsq", [128, D], F32))
        ln_bufs = (lred, lcx, lsq)
        s.emit("sp", lambda e: e.dma_start(out=eye_bf_s[:], in_=eye_bf[:]), writes=("eyeb",), inc=16)
        s.emit("sp", lambda e: e.dma_start(out=eye_f32_s[:], in_=eye_f32[:]), writes=("eyef",), inc=16)
        s.emit("sp", lambda e: e.dma_start(out=ones_bf_s[:], in_=ones_bf[:]), writes=("ones",), inc=16)
        s.emit("sp", lambda e: e.dma_start(out=ones_f32_s[:], in_=ones_f32[:]), writes=("onesf",), inc=16)

        with ExitStack() as _es:
            projT_bf = _es.enter_context(nc.sbuf_tensor("projT_bf", [128, 8 * D], BF16))
            gw_bf = _es.enter_context(nc.sbuf_tensor("gw_bf", [128, 256], BF16))
            pstg = _es.enter_context(nc.sbuf_tensor("pstg", [128, 4096], F32))
            pb_s = _es.enter_context(nc.sbuf_tensor("pb_s", [1, D], F32))
            pb_bf = _es.enter_context(nc.sbuf_tensor("pb_bf", [1, D], BF16))
            gb_s = _es.enter_context(nc.sbuf_tensor("gb_s", [1, 32], F32))
            gb_bc = _es.enter_context(nc.sbuf_tensor("gb_bc", [128, 32], F32))
            pt_bf = _es.enter_context(nc.sbuf_tensor("pt_bf", [128, D], BF16))
            pT_bf = _es.enter_context(nc.sbuf_tensor("pT_bf", [128, D], BF16))
            xo_s = _es.enter_context(nc.sbuf_tensor("xo_s", [128, D], F32))
            xr_s = _es.enter_context(nc.sbuf_tensor("xr_s", [128, D], F32))
            sc_s = _es.enter_context(nc.sbuf_tensor("sc_s", [128, 32], F32))
            scb_s = _es.enter_context(nc.sbuf_tensor("scb_s", [128, 32], F32))
            gwork = _es.enter_context(nc.sbuf_tensor("gwork", [128, 192], F32))
            psT0 = _es.enter_context(nc.psum_tensor("psT0", [128, 128], BF16))
            psT1 = _es.enter_context(nc.psum_tensor("psT1", [128, 128], BF16))
            psO0 = _es.enter_context(nc.psum_tensor("psO0", [128, 512], F32))
            psO1 = _es.enter_context(nc.psum_tensor("psO1", [128, 512], F32))
            psG = _es.enter_context(nc.psum_tensor("psG", [128, 32], F32))
            PWA, PWB, GS, VT, MASK, GMASK, EQ4 = 0, 8, 16, 24, 32, 64, 72
            MX, DEN, REC, REC2 = 80, 81, 82, 83
            VT32, EQ32, CRAW, SEL = 84, 116, 148, 0
            for half in range(2):
                for j in range(4):
                    s.emit("sp", lambda e, half=half, j=j: e.dma_start(
                        out=pstg[:, j * 1024:(j + 1) * 1024],
                        in_=projT[half * 512 + j * 128: half * 512 + (j + 1) * 128, :]),
                        writes=("pstg",), inc=16)
                s.emit("act", lambda e, half=half: e.activation(
                    out=projT_bf[:, half * 4096:(half + 1) * 4096], in_=pstg[:], func=AF.Copy),
                    reads=("pstg",), writes=("projw",))
            for j in range(8):
                s.emit("sp", lambda e, j=j: e.dma_start(
                    out=pstg[:, j * 32:(j + 1) * 32], in_=gatewT[j * 128:(j + 1) * 128, :]),
                    writes=("pstg",), inc=16)
            s.emit("act", lambda e: e.activation(out=gw_bf[:], in_=pstg[:, 0:256], func=AF.Copy),
                   reads=("pstg",), writes=("gw",))
            s.emit("sp", lambda e: e.dma_start(out=pb_s[:], in_=projb[:]), writes=("pbk",), inc=16)
            s.emit("act", lambda e: e.activation(out=pb_bf[:], in_=pb_s[:], func=AF.Copy),
                   reads=("pbk",), writes=("pbb",))
            s.emit("sp", lambda e: e.dma_start(out=gb_s[:], in_=gateb[:]), writes=("gbr",), inc=16)
            s.emit("pe", lambda e: e.matmul(psG[:], ones_f32_s[:], gb_s[:],
                                            start=True, stop=True, skip_group_check=True),
                   reads=("gbr", "onesf"), writes=("psG",))
            s.emit("dve", lambda e: e.tensor_copy(gb_bc[:], psG[:]),
                   reads=("psG",), writes=("gbc",))

            for st in range(NS):
                s.emit("sp", lambda e, st=st: e.dma_start(
                    out=pt_bf[:], in_=p_in[st * 128:(st + 1) * 128, :]),
                    writes=("pt",), inc=16)
                for ic in range(8):
                    psT = (psT0, psT1)[ic % 2]
                    kps = f"psT{ic % 2}"
                    s.emit("pe", lambda e, psT=psT, ic=ic: e.transpose(
                        psT[:], pt_bf[:, ic * 128:(ic + 1) * 128], eye_bf_s[:]),
                        reads=("pt", "eyeb"), writes=(kps,))
                    s.emit("dve", lambda e, psT=psT, ic=ic: e.tensor_copy(
                        pT_bf[:, ic * 128:(ic + 1) * 128], psT[:]),
                        reads=(kps,), writes=("pT",))
                s.emit("sp", lambda e, st=st: e.dma_start(
                    out=xo_s[:], in_=x_own[st * 128:(st + 1) * 128, :]),
                    writes=("xo",), inc=16)
                for og in range(2):
                    psO = (psO0, psO1)[og]
                    for ic in range(8):
                        s.emit("pe", lambda e, psO=psO, ic=ic, og=og: e.matmul(
                            psO[:], pT_bf[:, ic * 128:(ic + 1) * 128],
                            projT_bf[:, ic * 1024 + og * 512: ic * 1024 + (og + 1) * 512],
                            start=(ic == 0), stop=False, skip_group_check=True),
                            reads=("pT", "projw"), writes=(f"psO{og}",))
                    s.emit("pe", lambda e, psO=psO, og=og: e.matmul(
                        psO[:], ones_bf_s[:], pb_bf[:, og * 512:(og + 1) * 512],
                        start=False, stop=True, skip_group_check=True),
                        reads=("ones", "pbb"), writes=(f"psO{og}",))
                    s.emit("dve", lambda e, psO=psO, og=og: e.tensor_tensor(
                        xr_s[:, og * 512:(og + 1) * 512], xo_s[:, og * 512:(og + 1) * 512],
                        psO[:], OP.add),
                        reads=("xo", f"psO{og}"), writes=("xr",))
                emit_ln(s, ln_bufs, xr_s, "xr", pt_bf[:], "pt")
                for ic in range(8):
                    psT = (psT0, psT1)[ic % 2]
                    kps = f"psT{ic % 2}"
                    s.emit("pe", lambda e, psT=psT, ic=ic: e.transpose(
                        psT[:], pt_bf[:, ic * 128:(ic + 1) * 128], eye_bf_s[:]),
                        reads=("pt", "eyeb"), writes=(kps,))
                    s.emit("dve", lambda e, psT=psT, ic=ic, st=st: e.tensor_copy(
                        x1T[:, ic * TC + st * 128: ic * TC + (st + 1) * 128], psT[:]),
                        reads=(kps,), writes=("x1T",))
                for ic in range(8):
                    s.emit("pe", lambda e, ic=ic, st=st: e.matmul(
                        psG[:], x1T[:, ic * TC + st * 128: ic * TC + (st + 1) * 128],
                        gw_bf[:, ic * 32:(ic + 1) * 32],
                        start=(ic == 0), stop=(ic == 7), skip_group_check=True),
                        reads=("x1T", "gw"), writes=("psG",))
                s.emit("act", lambda e: e.activation(out=sc_s[:], in_=psG[:], func=AF.Sigmoid),
                       reads=("psG",), writes=("sck",))
                s.emit("dve", lambda e: e.tensor_tensor(scb_s[:], sc_s[:], gb_bc[:], OP.add),
                       reads=("sck", "gbc"), writes=("scb",))
                scb3 = scb_s.rearrange("p (g e) -> p g e", e=4)
                pairs = [(0, 1), (0, 2), (0, 3), (1, 2), (1, 3), (2, 3)]
                for pi, (i1, i2) in enumerate(pairs):
                    dst = gwork[:, PWB:PWB + 8] if pi else gwork[:, GS:GS + 8]
                    dkey = "pwB" if pi else "gsk"
                    s.emit("dve", lambda e, i1=i1, i2=i2, dst=dst: e.tensor_tensor(
                        dst, scb3[:, :, i1], scb3[:, :, i2], OP.add),
                        reads=("scb",), writes=(dkey,))
                    if pi:
                        s.emit("dve", lambda e: e.tensor_tensor(
                            gwork[:, GS:GS + 8], gwork[:, GS:GS + 8], gwork[:, PWB:PWB + 8], OP.max),
                            reads=("gsk", "pwB"), writes=("gsk",))
                s.emit("dve", lambda e: e.tensor_copy(gwork[:, VT:VT + 8], gwork[:, GS:GS + 8]),
                       reads=("gsk",), writes=("vtk",))
                for it in range(3):
                    s.emit("dve", lambda e: e.reduce_max(
                        gwork[:, MX:MX + 1], gwork[:, VT:VT + 8], axis=AXX),
                        reads=("vtk",), writes=("mxk",))
                    s.emit("dve", lambda e: e.tensor_scalar(
                        gwork[:, EQ4:EQ4 + 8], gwork[:, VT:VT + 8], gwork[:, MX:MX + 1],
                        None, OP.is_ge),
                        reads=("vtk", "mxk"), writes=("eq4k",))
                    s.emit("dve", lambda e: e.scalar_tensor_tensor(
                        gwork[:, VT:VT + 8], gwork[:, EQ4:EQ4 + 8], -1e30, gwork[:, VT:VT + 8],
                        OP.mult, OP.add),
                        reads=("vtk", "eq4k"), writes=("vtk",))
                s.emit("dve", lambda e: e.reduce_max(
                    gwork[:, MX:MX + 1], gwork[:, VT:VT + 8], axis=AXX),
                    reads=("vtk",), writes=("mxk",))
                s.emit("dve", lambda e: e.tensor_scalar(
                    gwork[:, GMASK:GMASK + 8], gwork[:, GS:GS + 8], gwork[:, MX:MX + 1],
                    None, OP.is_ge),
                    reads=("gsk", "mxk"), writes=("gmk",))
                mk = gwork[:, MASK:MASK + 32].rearrange("p (g e) -> p g e", e=4)
                for i in range(4):
                    s.emit("dve", lambda e, i=i: e.tensor_copy(
                        mk[:, :, i], gwork[:, GMASK:GMASK + 8]),
                        reads=("gmk",), writes=("smk",))
                s.emit("dve", lambda e: e.tensor_tensor(
                    gwork[:, MASK:MASK + 32], gwork[:, MASK:MASK + 32], scb_s[:], OP.mult),
                    reads=("smk", "scb"), writes=("smk",))
                s.emit("dve", lambda e: e.tensor_copy(
                    gwork[:, VT32:VT32 + 32], gwork[:, MASK:MASK + 32]),
                    reads=("smk",), writes=("vt32k",))
                for it in range(7):
                    s.emit("dve", lambda e: e.reduce_max(
                        gwork[:, MX:MX + 1], gwork[:, VT32:VT32 + 32], axis=AXX),
                        reads=("vt32k",), writes=("mxk",))
                    s.emit("dve", lambda e: e.tensor_scalar(
                        gwork[:, EQ32:EQ32 + 32], gwork[:, VT32:VT32 + 32], gwork[:, MX:MX + 1],
                        None, OP.is_ge),
                        reads=("vt32k", "mxk"), writes=("eq32k",))
                    s.emit("dve", lambda e: e.scalar_tensor_tensor(
                        gwork[:, VT32:VT32 + 32], gwork[:, EQ32:EQ32 + 32], -1e30,
                        gwork[:, VT32:VT32 + 32], OP.mult, OP.add),
                        reads=("vt32k", "eq32k"), writes=("vt32k",))
                s.emit("dve", lambda e: e.reduce_max(
                    gwork[:, MX:MX + 1], gwork[:, VT32:VT32 + 32], axis=AXX),
                    reads=("vt32k",), writes=("mxk",))
                s.emit("dve", lambda e: e.tensor_scalar(
                    gwork[:, EQ32:EQ32 + 32], gwork[:, MASK:MASK + 32], gwork[:, MX:MX + 1],
                    None, OP.is_ge),
                    reads=("smk", "mxk"), writes=("selk",))
                s.emit("dve", lambda e: e.tensor_tensor(
                    gwork[:, CRAW:CRAW + 32], gwork[:, EQ32:EQ32 + 32], sc_s[:], OP.mult),
                    reads=("selk", "sck"), writes=("crawk",))
                s.emit("dve", lambda e: e.reduce_sum(
                    gwork[:, DEN:DEN + 1], gwork[:, CRAW:CRAW + 32], axis=AXX),
                    reads=("crawk",), writes=("denk",))
                s.emit("dve", lambda e: e.tensor_scalar(
                    gwork[:, REC:REC + 1], gwork[:, DEN:DEN + 1], 1e-20, None, OP.add),
                    reads=("denk",), writes=("reck",))
                s.emit("dve", lambda e: e.reciprocal(gwork[:, REC2:REC2 + 1], gwork[:, REC:REC + 1]),
                       reads=("reck",), writes=("rec2k",))
                s.emit("dve", lambda e: e.tensor_scalar(
                    gwork[:, REC2:REC2 + 1], gwork[:, REC2:REC2 + 1], 2.5, None, OP.mult),
                    reads=("rec2k",), writes=("rec2k",))
                s.emit("dve", lambda e, st=st: e.tensor_scalar(
                    cmb[:, st * 34: st * 34 + 32], gwork[:, CRAW:CRAW + 32],
                    gwork[:, REC2:REC2 + 1], None, OP.mult),
                    reads=("crawk", "rec2k"), writes=(f"cmb{st}",))
                s.emit("dve", lambda e, st=st: e.memset(cmb[:, st * 34 + 32: st * 34 + 34], 1.0),
                       writes=(f"cmbs{st}",))
        s.fence()

        with ExitStack() as _es:
            wg_bf0 = _es.enter_context(nc.sbuf_tensor("wg_bf0", [128, 4096], BF16))
            wg_bf1 = _es.enter_context(nc.sbuf_tensor("wg_bf1", [128, 4096], BF16))
            wu_bf0 = _es.enter_context(nc.sbuf_tensor("wu_bf0", [128, 4096], BF16))
            wu_bf1 = _es.enter_context(nc.sbuf_tensor("wu_bf1", [128, 4096], BF16))
            wd_bf0 = _es.enter_context(nc.sbuf_tensor("wd_bf0", [128, 4096], BF16))
            wd_bf1 = _es.enter_context(nc.sbuf_tensor("wd_bf1", [128, 4096], BF16))
            wstgA = _es.enter_context(nc.sbuf_tensor("wstgA", [128, 2048], F32))
            wstgB = _es.enter_context(nc.sbuf_tensor("wstgB", [128, 2048], F32))
            cmrow = _es.enter_context(nc.sbuf_tensor("cmrow", [1, TC], F32))
            cmrow_bf = _es.enter_context(nc.sbuf_tensor("cmrow_bf", [1, TC], BF16))
            cmbc = _es.enter_context(nc.sbuf_tensor("cmbc", [128, 512], F32))
            gs_sb = _es.enter_context(nc.sbuf_tensor("gs_sb", [128, 512], F32))
            us_sb = _es.enter_context(nc.sbuf_tensor("us_sb", [128, 512], F32))
            t1_sb = _es.enter_context(nc.sbuf_tensor("t1_sb", [128, 512], F32))
            guT = _es.enter_context(nc.sbuf_tensor("guT", [128, 4 * 512], BF16))
            psG2 = _es.enter_context(nc.psum_tensor("psG2", [128, 512], F32))
            psU2 = _es.enter_context(nc.psum_tensor("psU2", [128, 512], F32))
            psD0 = _es.enter_context(nc.psum_tensor("psD0", [128, 512], F32))
            psD1 = _es.enter_context(nc.psum_tensor("psD1", [128, 512], F32))
            psB = _es.enter_context(nc.psum_tensor("psB", [128, 512], F32))
            psCT2 = _es.enter_context(nc.psum_tensor("psCT2", [1, 128], F32))
            NEXP = 34
            for ei in range(NEXP):
                bsel = ei % 2
                wg_bf = (wg_bf0, wg_bf1)[bsel]
                wu_bf = (wu_bf0, wu_bf1)[bsel]
                wd_bf = (wd_bf0, wd_bf1)[bsel]
                kg, ku, kd = f"wg{bsel}", f"wu{bsel}", f"wd{bsel}"
                half = ei - 32
                for kind, dst, key in (("g", wg_bf, kg), ("u", wu_bf, ku)):
                    for h8 in range(2):
                        stg, skey = ((wstgA, f"wsA{bsel}"), (wstgB, f"wsB{bsel}"))[h8]
                        for j in range(4):
                            rt = h8 * 4 + j
                            if ei < 32:
                                src = (wgT if kind == "g" else wuT)[ei, rt * 128:(rt + 1) * 128, :]
                            else:
                                src = (shgT if kind == "g" else shuT)[
                                    rt * 128:(rt + 1) * 128, half * 512:(half + 1) * 512]
                            s.emit("sp", lambda e, src=src, j=j, stg=stg: e.dma_start(
                                out=stg[:, j * 512:(j + 1) * 512], in_=src),
                                writes=(skey,), inc=16)
                        s.emit("act", lambda e, dst=dst, stg=stg, h8=h8: e.activation(
                            out=dst[:, h8 * 2048:(h8 + 1) * 2048], in_=stg[:], func=AF.Copy),
                            reads=(skey,), writes=(key,))
                for h8 in range(2):
                    stg, skey = ((wstgA, f"wsA{bsel}"), (wstgB, f"wsB{bsel}"))[h8]
                    for j in range(2):
                        rt = h8 * 2 + j
                        if ei < 32:
                            src = wdT[ei, rt * 128:(rt + 1) * 128, :]
                        else:
                            src = shdT[half * 512 + rt * 128: half * 512 + (rt + 1) * 128, :]
                        s.emit("sp", lambda e, src=src, j=j, stg=stg: e.dma_start(
                            out=stg[:, j * 1024:(j + 1) * 1024], in_=src),
                            writes=(skey,), inc=16)
                    s.emit("act", lambda e, stg=stg, h8=h8, wd_bf=wd_bf: e.activation(
                        out=wd_bf[:, h8 * 2048:(h8 + 1) * 2048], in_=stg[:], func=AF.Copy),
                        reads=(skey,), writes=(kd,))
                for st in range(NS):
                    s.emit("pe", lambda e, st=st, ei=ei: e.transpose(
                        psCT2[:], cmb[:, st * 34 + ei: st * 34 + ei + 1], eye_f32_s[:]),
                        reads=(f"cmb{st}", f"cmbs{st}", "eyef"), writes=("psCT2",))
                    s.emit("dve", lambda e, st=st: e.tensor_copy(
                        cmrow[:, st * 128:(st + 1) * 128], psCT2[0:1, :]),
                        reads=("psCT2",), writes=("cmrow",))
                s.emit("act", lambda e: e.activation(out=cmrow_bf[:], in_=cmrow[:], func=AF.Copy),
                       reads=("cmrow",), writes=("cmrowb",))
                for tg in range(4):
                    s.emit("pe", lambda e, tg=tg: e.matmul(
                        psB[:], ones_bf_s[:], cmrow_bf[:, tg * 512:(tg + 1) * 512],
                        start=True, stop=True, skip_group_check=True),
                        reads=("cmrowb", "ones"), writes=("psB",))
                    s.emit("dve", lambda e: e.tensor_copy(cmbc[:], psB[:]),
                           reads=("psB",), writes=("cmbck",))
                    for it in range(4):
                        for ic in range(8):
                            s.emit("pe", lambda e, it=it, ic=ic, tg=tg, wg_bf=wg_bf: e.matmul(
                                psG2[:], wg_bf[:, ic * 512 + it * 128: ic * 512 + (it + 1) * 128],
                                x1T[:, ic * TC + tg * 512: ic * TC + (tg + 1) * 512],
                                start=(ic == 0), stop=(ic == 7), skip_group_check=True),
                                reads=(kg, "x1T"), writes=("psG2",))
                        for ic in range(8):
                            s.emit("pe", lambda e, it=it, ic=ic, tg=tg, wu_bf=wu_bf: e.matmul(
                                psU2[:], wu_bf[:, ic * 512 + it * 128: ic * 512 + (it + 1) * 128],
                                x1T[:, ic * TC + tg * 512: ic * TC + (tg + 1) * 512],
                                start=(ic == 0), stop=(ic == 7), skip_group_check=True),
                                reads=(ku, "x1T"), writes=("psU2",))
                        s.emit("act", lambda e: e.activation(out=gs_sb[:], in_=psG2[:], func=AF.Silu),
                               reads=("psG2",), writes=("gssb",))
                        s.emit("act", lambda e: e.activation(out=us_sb[:], in_=psU2[:], func=AF.Copy),
                               reads=("psU2",), writes=("ussb",))
                        s.emit("pool", lambda e: e.tensor_tensor(
                            t1_sb[:], gs_sb[:], us_sb[:], OP.mult),
                            reads=("gssb", "ussb"), writes=("t1sb",))
                        s.emit("pool", lambda e, it=it: e.tensor_tensor(
                            guT[:, it * 512:(it + 1) * 512], t1_sb[:], cmbc[:], OP.mult),
                            reads=("t1sb", "cmbck"), writes=(f"guT{it}",))
                    for sl in range(4):
                        stt = tg * 4 + sl
                        for og in range(2):
                            psD = (psD0, psD1)[og]
                            for it in range(4):
                                s.emit("pe", lambda e, it=it, sl=sl, og=og, psD=psD,
                                       wd_bf=wd_bf: e.matmul(
                                    psD[:], guT[:, it * 512 + sl * 128: it * 512 + (sl + 1) * 128],
                                    wd_bf[:, it * 1024 + og * 512: it * 1024 + (og + 1) * 512],
                                    start=(it == 0), stop=(it == 3), skip_group_check=True),
                                    reads=(kd, "guT0", "guT1", "guT2", "guT3"),
                                    writes=(f"psD{og}",))
                            ac = accum[:, stt * 1024 + og * 512: stt * 1024 + (og + 1) * 512]
                            if ei == 0:
                                s.emit("dve", lambda e, ac=ac, psD=psD: e.tensor_copy(ac, psD[:]),
                                       reads=(f"psD{og}",), writes=(f"acc{stt}_{og}",))
                            else:
                                s.emit("dve", lambda e, ac=ac, psD=psD: e.tensor_tensor(
                                    ac, ac, psD[:], OP.add),
                                    reads=(f"psD{og}", f"acc{stt}_{og}"),
                                    writes=(f"acc{stt}_{og}",))
        s.fence()
        with ExitStack() as _es:
            xc_s = _es.enter_context(nc.sbuf_tensor("xc_s", [128, D], F32))
            xr2_s = _es.enter_context(nc.sbuf_tensor("xr2_s", [128, D], F32))
            x2t_s = _es.enter_context(nc.sbuf_tensor("x2t_s", [128, D], F32))
            psT2 = _es.enter_context(nc.psum_tensor("psT2", [128, 128], BF16))
            for st in range(NS):
                for ic in range(8):
                    s.emit("pe", lambda e, ic=ic, st=st: e.transpose(
                        psT2[:], x1T[:, ic * TC + st * 128: ic * TC + (st + 1) * 128],
                        eye_bf_s[:]),
                        reads=("x1T", "eyeb"), writes=("psT2",))
                    s.emit("act", lambda e, ic=ic: e.activation(
                        out=xc_s[:, ic * 128:(ic + 1) * 128], in_=psT2[:], func=AF.Copy),
                        reads=("psT2",), writes=("xck",))
                s.emit("dve", lambda e, st=st: e.tensor_tensor(
                    xr2_s[:], xc_s[:], accum[:, st * 1024:(st + 1) * 1024], OP.add),
                    reads=("xck", f"acc{st}_0", f"acc{st}_1"), writes=("xr2",))
                emit_ln(s, (lred, lcx, lsq), xr2_s, "xr2", x2t_s[:], "x2t")
                s.emit("pool", lambda e, st=st: e.dma_start(
                    out=x2_sh[st * 128:(st + 1) * 128, :], in_=x2t_s[:]),
                    reads=("x2t",), inc=16)
        s.finish()
        s.run(nc)
    return nc



def build_launch3():
    nc = bass.Bass("TRN2", target_bir_lowering=False)
    xkv = nc.dram_tensor("xkv", [NSEQ, D], F32, kind="ExternalInput").ap()
    wq = nc.dram_tensor("wq", [D, D], F32, kind="ExternalInput").ap()
    wk = nc.dram_tensor("wk", [D, D], F32, kind="ExternalInput").ap()
    wv = nc.dram_tensor("wv", [D, D], F32, kind="ExternalInput").ap()
    bq_c = nc.dram_tensor("bq_c", [128, 8], F32, kind="ExternalInput").ap()
    bk_c = nc.dram_tensor("bk_c", [128, 8], F32, kind="ExternalInput").ap()
    bv_row = nc.dram_tensor("bv_row", [1, D], F32, kind="ExternalInput").ap()
    fbT = nc.dram_tensor("fbT", [16, 16], F32, kind="ExternalInput").ap()
    vmat = nc.dram_tensor("vmat", [16, D], F32, kind="ExternalInput").ap()
    uT = nc.dram_tensor("uT", [16, 4096], F32, kind="ExternalInput").ap()
    gate = nc.dram_tensor("gate", [1, 1], F32, kind="ExternalInput").ap()
    wfc = nc.dram_tensor("wfc", [4096, D], F32, kind="ExternalInput").ap()
    wproj = nc.dram_tensor("wproj", [D, 4096], F32, kind="ExternalInput").ap()
    eye_bf = nc.dram_tensor("eye_bf", [128, 128], BF16, kind="ExternalInput").ap()
    eye_f32 = nc.dram_tensor("eye_f32", [128, 128], F32, kind="ExternalInput").ap()
    ones_bf = nc.dram_tensor("ones_bf", [1, 128], BF16, kind="ExternalInput").ap()
    ones_f32 = nc.dram_tensor("ones_f32", [1, 128], F32, kind="ExternalInput").ap()
    out_sh = nc.dram_tensor("out_sh", [TC, D], F32, kind="ExternalOutput").ap()
    x3_dram = nc.dram_tensor("x3_scratch", [TC, D], F32, kind="Internal").ap()

    s = Sched()
    with ExitStack() as _es:
        eye_bf_s = _es.enter_context(nc.sbuf_tensor("eye_bf_s", [128, 128], BF16))
        eye_f32_s = _es.enter_context(nc.sbuf_tensor("eye_f32_s", [128, 128], F32))
        ones_bf_s = _es.enter_context(nc.sbuf_tensor("ones_bf_s", [1, 128], BF16))
        ones_f32_s = _es.enter_context(nc.sbuf_tensor("ones_f32_s", [1, 128], F32))
        lred = _es.enter_context(nc.sbuf_tensor("lred", [128, 4], F32))
        lcx = _es.enter_context(nc.sbuf_tensor("lcx", [128, D], F32))
        lsq = _es.enter_context(nc.sbuf_tensor("lsq", [128, D], F32))
        ln_bufs = (lred, lcx, lsq)
        s.emit("sp", lambda e: e.dma_start(out=eye_bf_s[:], in_=eye_bf[:]), writes=("eyeb",), inc=16)
        s.emit("sp", lambda e: e.dma_start(out=eye_f32_s[:], in_=eye_f32[:]), writes=("eyef",), inc=16)
        s.emit("sp", lambda e: e.dma_start(out=ones_bf_s[:], in_=ones_bf[:]), writes=("ones",), inc=16)
        s.emit("sp", lambda e: e.dma_start(out=ones_f32_s[:], in_=ones_f32[:]), writes=("onesf",), inc=16)

        with ExitStack() as _es:
            x2T = _es.enter_context(nc.sbuf_tensor("x2T", [128, 8 * NSEQ], BF16))
            attnout = _es.enter_context(nc.sbuf_tensor("attnout", [128, NS * D], BF16))
            with ExitStack() as _es:
                xstg0 = _es.enter_context(nc.sbuf_tensor("xstg0", [128, D], F32))
                xstg1 = _es.enter_context(nc.sbuf_tensor("xstg1", [128, D], F32))
                xb0 = _es.enter_context(nc.sbuf_tensor("xb0", [128, D], BF16))
                xb1 = _es.enter_context(nc.sbuf_tensor("xb1", [128, D], BF16))
                psTa = _es.enter_context(nc.psum_tensor("psTa", [128, 128], BF16))
                psTb = _es.enter_context(nc.psum_tensor("psTb", [128, 128], BF16))
                for j in range(32):
                    xstg = (xstg0, xstg1)[j % 2]
                    xb = (xb0, xb1)[j % 2]
                    s.emit("sp", lambda e, j=j, xstg=xstg: e.dma_start(
                        out=xstg[:], in_=xkv[j * 128:(j + 1) * 128, :]),
                        writes=(f"xstg{j % 2}",), inc=16)
                    s.emit("act", lambda e, xb=xb, xstg=xstg, j=j: e.activation(
                        out=xb[:], in_=xstg[:], func=AF.Copy),
                        reads=(f"xstg{j % 2}",), writes=(f"xb{j % 2}",))
                    for ic in range(8):
                        psT = (psTa, psTb)[ic % 2]
                        kps = f"psTx{ic % 2}"
                        s.emit("pe", lambda e, psT=psT, xb=xb, ic=ic, j=j: e.transpose(
                            psT[:], xb[:, ic * 128:(ic + 1) * 128], eye_bf_s[:]),
                            reads=(f"xb{j % 2}", "eyeb"), writes=(kps,))
                        s.emit("dve", lambda e, psT=psT, ic=ic, j=j: e.tensor_copy(
                            x2T[:, ic * NSEQ + j * 128: ic * NSEQ + (j + 1) * 128], psT[:]),
                            reads=(kps,), writes=("x2T",))
            s.fence()
            with ExitStack() as _es:
                wq_bf = _es.enter_context(nc.sbuf_tensor("wq_bf", [128, 2048], BF16))
                wk_bf = _es.enter_context(nc.sbuf_tensor("wk_bf", [128, 2048], BF16))
                wv_bf = _es.enter_context(nc.sbuf_tensor("wv_bf", [128, 2048], BF16))
                wstg3 = _es.enter_context(nc.sbuf_tensor("wstg3", [128, 2048], F32))
                bqc_s = _es.enter_context(nc.sbuf_tensor("bqc_s", [128, 8], F32))
                bkc_s = _es.enter_context(nc.sbuf_tensor("bkc_s", [128, 8], F32))
                bvr_s = _es.enter_context(nc.sbuf_tensor("bvr_s", [1, D], F32))
                bvr_bf = _es.enter_context(nc.sbuf_tensor("bvr_bf", [1, D], BF16))
                qT_hg = _es.enter_context(nc.sbuf_tensor("qT_hg", [128, 2 * TC], BF16))
                kT_hg = _es.enter_context(nc.sbuf_tensor("kT_hg", [128, 2 * NSEQ], BF16))
                v_hg = _es.enter_context(nc.sbuf_tensor("v_hg", [128, 32 * 260], BF16))
                esp3 = _es.enter_context(nc.sbuf_tensor("esp3", [128, 1024], BF16))
                av_s3 = _es.enter_context(nc.sbuf_tensor("av_s3", [65, 512], F32))
                avT_s3 = _es.enter_context(nc.sbuf_tensor("avT_s3", [128, 65], F32))
                psQ3 = _es.enter_context(nc.psum_tensor("psQ3", [128, 512], F32))
                psV3 = _es.enter_context(nc.psum_tensor("psV3", [128, 256], F32))
                psS0 = _es.enter_context(nc.psum_tensor("psS0", [128, 512], F32))
                psS1 = _es.enter_context(nc.psum_tensor("psS1", [128, 512], F32))
                psAV = _es.enter_context(nc.psum_tensor("psAV", [65, 512], F32))
                psAVT = _es.enter_context(nc.psum_tensor("psAVT", [128, 65], F32))
                s.emit("sp", lambda e: e.dma_start(out=bqc_s[:], in_=bq_c[:]), writes=("bqc",), inc=16)
                s.emit("sp", lambda e: e.dma_start(out=bkc_s[:], in_=bk_c[:]), writes=("bkc",), inc=16)
                s.emit("sp", lambda e: e.dma_start(out=bvr_s[:], in_=bv_row[:]), writes=("bvr",), inc=16)
                s.emit("act", lambda e: e.activation(out=bvr_bf[:], in_=bvr_s[:], func=AF.Copy),
                       reads=("bvr",), writes=("bvrb",))
                for hg in range(4):
                    for wsrc_, wdst, wkey in ((wq, wq_bf, "wq3"), (wk, wk_bf, "wk3"),
                                              (wv, wv_bf, "wv3")):
                        for rt in range(8):
                            s.emit("sp", lambda e, wsrc_=wsrc_, rt=rt, hg=hg: e.dma_start(
                                out=wstg3[:, rt * 256:(rt + 1) * 256],
                                in_=wsrc_[rt * 128:(rt + 1) * 128, hg * 256:(hg + 1) * 256]),
                                writes=("wstg3",), inc=16)
                        s.emit("act", lambda e, wdst=wdst: e.activation(
                            out=wdst[:], in_=wstg3[:], func=AF.Copy),
                            reads=("wstg3",), writes=(wkey,))
                    s.emit("dve", lambda e: e.memset(v_hg[:], 1.0), writes=("vhgi",),
                           extra_deps=[],
                           )
                    for ft in range(2):
                        for tg in range(4):
                            for ic in range(8):
                                s.emit("pe", lambda e, ft=ft, tg=tg, ic=ic: e.matmul(
                                    psQ3[:], wq_bf[:, ic * 256 + ft * 128: ic * 256 + (ft + 1) * 128],
                                    x2T[:, ic * NSEQ + tg * 512: ic * NSEQ + (tg + 1) * 512],
                                    start=(ic == 0), stop=(ic == 7), skip_group_check=True),
                                    reads=("wq3", "x2T"), writes=("psQ3",))
                            s.emit("dve", lambda e, ft=ft, tg=tg, hg=hg: e.tensor_scalar(
                                qT_hg[:, ft * TC + tg * 512: ft * TC + (tg + 1) * 512],
                                psQ3[:], bqc_s[:, hg * 2 + ft: hg * 2 + ft + 1], None, OP.add),
                                reads=("psQ3", "bqc"), writes=("qThg",))
                    for ft in range(2):
                        for tg in range(8):
                            for ic in range(8):
                                s.emit("pe", lambda e, ft=ft, tg=tg, ic=ic: e.matmul(
                                    psQ3[:], wk_bf[:, ic * 256 + ft * 128: ic * 256 + (ft + 1) * 128],
                                    x2T[:, ic * NSEQ + tg * 512: ic * NSEQ + (tg + 1) * 512],
                                    start=(ic == 0), stop=(ic == 7), skip_group_check=True),
                                    reads=("wk3", "x2T"), writes=("psQ3",))
                            s.emit("dve", lambda e, ft=ft, tg=tg, hg=hg: e.tensor_scalar(
                                kT_hg[:, ft * NSEQ + tg * 512: ft * NSEQ + (tg + 1) * 512],
                                psQ3[:], bkc_s[:, hg * 2 + ft: hg * 2 + ft + 1], None, OP.add),
                                reads=("psQ3", "bkc"), writes=("kThg",))
                    for st in range(32):
                        for ic in range(8):
                            s.emit("pe", lambda e, st=st, ic=ic: e.matmul(
                                psV3[:], x2T[:, ic * NSEQ + st * 128: ic * NSEQ + (st + 1) * 128],
                                wv_bf[:, ic * 256:(ic + 1) * 256],
                                start=(ic == 0), stop=False, skip_group_check=True),
                                reads=("wv3", "x2T"), writes=("psV3",))
                        s.emit("pe", lambda e, hg=hg: e.matmul(
                            psV3[:], ones_bf_s[:], bvr_bf[:, hg * 256:(hg + 1) * 256],
                            start=False, stop=True, skip_group_check=True),
                            reads=("ones", "bvrb"), writes=("psV3",))
                        for j in range(4):
                            s.emit("dve", lambda e, st=st, j=j: e.tensor_copy(
                                v_hg[:, st * 260 + j * 65: st * 260 + j * 65 + 64],
                                psV3[:, j * 64:(j + 1) * 64]),
                                reads=("psV3", "vhgi"), writes=(f"vh{st}_{j}",))
                    for j in range(4):
                        r0 = (j % 2) * 64
                        ft = j // 2
                        for qg in range(4):
                            def mm_s3(e, kc, ps, r0=r0, ft=ft, qg=qg):
                                return e.matmul(
                                    ps[:],
                                    kT_hg[r0:r0 + 64,
                                          ft * NSEQ + kc * 128: ft * NSEQ + (kc + 1) * 128],
                                    qT_hg[r0:r0 + 64,
                                          ft * TC + qg * 512: ft * TC + (qg + 1) * 512],
                                    start=True, stop=True, skip_group_check=True)

                            s.emit("pe", lambda e, f=mm_s3: f(e, 0, psS0),
                                   reads=("qThg", "kThg"), writes=("psS0",))
                            for kc in range(32):
                                ps = (psS0, psS1)[kc % 2]
                                s.emit("act", lambda e, kc=kc, ps=ps: e.activation(
                                    out=esp3[:, (kc % 2) * 512:(kc % 2 + 1) * 512], in_=ps[:],
                                    func=AF.Exp, scale=0.125),
                                    reads=(f"psS{kc % 2}",), writes=(f"esp{kc % 2}",))
                                if kc < 31:
                                    nps = (psS0, psS1)[(kc + 1) % 2]
                                    s.emit("pe", lambda e, f=mm_s3, kc=kc, nps=nps: f(e, kc + 1, nps),
                                           reads=("qThg", "kThg"),
                                           writes=(f"psS{(kc + 1) % 2}",))
                                s.emit("pe", lambda e, kc=kc, j=j: e.matmul(
                                    psAV[:], v_hg[:, kc * 260 + j * 65: kc * 260 + j * 65 + 65],
                                    esp3[:, (kc % 2) * 512:(kc % 2 + 1) * 512],
                                    start=(kc == 0), stop=(kc == 31), skip_group_check=True),
                                    reads=(f"esp{kc % 2}", f"vh{kc}_{j}"), writes=("psAV",))
                            s.emit("dve", lambda e: e.tensor_copy(av_s3[:], psAV[:]),
                                   reads=("psAV",), writes=("av_s3",))
                            for j2 in range(4):
                                stt = qg * 4 + j2
                                hcol = (hg * 4 + j) * 64
                                s.emit("pe", lambda e, j2=j2: e.transpose(
                                    psAVT[:], av_s3[:, j2 * 128:(j2 + 1) * 128],
                                    eye_f32_s[0:65, 0:65]),
                                    reads=("av_s3", "eyef"), writes=("psAVT",))
                                s.emit("dve", lambda e: e.tensor_copy(avT_s3[:], psAVT[:]),
                                       reads=("psAVT",), writes=("avT_s3",))
                                s.emit("dve", lambda e: e.reciprocal(
                                    avT_s3[:, 64:65], avT_s3[:, 64:65]),
                                    reads=("avT_s3",), writes=("avT_s3",))
                                s.emit("dve", lambda e, stt=stt, hcol=hcol: e.tensor_scalar(
                                    attnout[:, stt * 1024 + hcol: stt * 1024 + hcol + 64],
                                    avT_s3[:, 0:64], avT_s3[:, 64:65], None, OP.mult),
                                    reads=("avT_s3",), writes=(f"ao{stt}_{hcol}",))
            s.fence()
            with ExitStack() as _es:
                xo3 = _es.enter_context(nc.sbuf_tensor("xo3", [128, D], F32))
                ac3 = _es.enter_context(nc.sbuf_tensor("ac3", [128, D], F32))
                xr3 = _es.enter_context(nc.sbuf_tensor("xr3", [128, D], F32))
                x3t = _es.enter_context(nc.sbuf_tensor("x3t", [128, D], F32))
                for st in range(NS):
                    s.emit("sp", lambda e, st=st: e.dma_start(
                        out=xo3[:], in_=xkv[st * 128:(st + 1) * 128, :]),
                        writes=("xo3",), inc=16)
                    s.emit("act", lambda e, st=st: e.activation(
                        out=ac3[:], in_=attnout[:, st * 1024:(st + 1) * 1024], func=AF.Copy),
                        reads=tuple(f"ao{st}_{h * 64}" for h in range(16)), writes=("ac3",))
                    s.emit("dve", lambda e: e.tensor_tensor(xr3[:], xo3[:], ac3[:], OP.add),
                           reads=("xo3", "ac3"), writes=("xr3",))
                    emit_ln(s, ln_bufs, xr3, "xr3", x3t[:], "x3t")
                    s.emit("pool", lambda e, st=st: e.dma_start(
                        out=x3_dram[st * 128:(st + 1) * 128, :], in_=x3t[:]),
                        reads=("x3t",), inc=16)
            s.fence()
        with ExitStack() as _es:
            weffT = _es.enter_context(nc.sbuf_tensor("weffT", [128, 8 * 4096], BF16))
            wprojT = _es.enter_context(nc.sbuf_tensor("wprojT", [128, 32 * 1024], BF16))
            fstg0 = _es.enter_context(nc.sbuf_tensor("fstg0", [128, 1024], F32))
            fstg1 = _es.enter_context(nc.sbuf_tensor("fstg1", [128, 1024], F32))
            tw0 = _es.enter_context(nc.sbuf_tensor("tw0", [128, 1024], F32))
            tw1 = _es.enter_context(nc.sbuf_tensor("tw1", [128, 1024], F32))
            wef_bf = _es.enter_context(nc.sbuf_tensor("wef_bf", [128, 1024], BF16))
            sred = _es.enter_context(nc.sbuf_tensor("sred", [128, 4], F32))
            fbT_s = _es.enter_context(nc.sbuf_tensor("fbT_s", [16, 16], F32))
            fbT_bf = _es.enter_context(nc.sbuf_tensor("fbT_bf", [16, 16], BF16))
            vm_bf = _es.enter_context(nc.sbuf_tensor("vm_bf", [16, D], BF16))
            uT_bf = _es.enter_context(nc.sbuf_tensor("uT_bf", [16, 4096], BF16))
            A_bf = _es.enter_context(nc.sbuf_tensor("A_bf", [16, D], BF16))
            gt_s = _es.enter_context(nc.sbuf_tensor("gt_s", [1, 1], F32))
            gt_col = _es.enter_context(nc.sbuf_tensor("gt_col", [128, 1], F32))
            x3bf = _es.enter_context(nc.sbuf_tensor("x3bf", [128, D], BF16))
            x3f = _es.enter_context(nc.sbuf_tensor("x3f", [128, D], F32))
            x3T_st = _es.enter_context(nc.sbuf_tensor("x3T_st", [128, D], BF16))
            rl_sb = _es.enter_context(nc.sbuf_tensor("rl_sb", [128, 512], F32))
            rr_bf = _es.enter_context(nc.sbuf_tensor("rr_bf", [128, 4096], BF16))
            rrT = _es.enter_context(nc.sbuf_tensor("rrT", [128, 4096], BF16))
            outt = _es.enter_context(nc.sbuf_tensor("outt", [128, D], F32))
            psT3 = _es.enter_context(nc.psum_tensor("psT3", [128, 128], BF16))
            psT4 = _es.enter_context(nc.psum_tensor("psT4", [128, 512], BF16))
            psA = _es.enter_context(nc.psum_tensor("psA", [16, 512], F32))
            psM = _es.enter_context(nc.psum_tensor("psM", [128, 512], F32))
            psH = _es.enter_context(nc.psum_tensor("psH", [128, 512], F32))
            psO3 = _es.enter_context(nc.psum_tensor("psO3", [128, 512], F32))
            for rt in range(32):
                fstg = (fstg0, fstg1)[rt % 2]
                kstg = f"fstg{rt % 2}"
                s.emit("sp", lambda e, rt=rt, fstg=fstg: e.dma_start(
                    out=fstg[:], in_=wfc[rt * 128:(rt + 1) * 128, :]),
                    writes=(kstg,), inc=16)
                s.emit("dve", lambda e, fstg=fstg: e.reduce_sum(
                    sred[:, 0:1], fstg[:], axis=AXX, apply_absolute_value=True),
                    reads=(kstg,), writes=("sr0",))
                s.emit("dve", lambda e: e.tensor_scalar(
                    sred[:, 1:2], sred[:, 0:1], 1.0 / 1024, 1e-5, OP.mult, OP.max),
                    reads=("sr0",), writes=("sr1",))
                s.emit("dve", lambda e: e.reciprocal(sred[:, 2:3], sred[:, 1:2]),
                       reads=("sr1",), writes=("sr2",))
                for qq in range(4):
                    sl = slice(qq * 256, (qq + 1) * 256)
                    s.emit("pool", lambda e, sl=sl, fstg=fstg: e.tensor_scalar(
                        tw0[:, 0:256], fstg[:, sl], sred[:, 2:3], None, OP.mult),
                        reads=(kstg, "sr2"), writes=("tw0",))
                    s.emit("pool", lambda e: e.tensor_scalar(
                        tw1[:, 0:256], tw0[:, 0:256], 0.5, None, OP.is_gt),
                        reads=("tw0",), writes=("tw1",))
                    s.emit("pool", lambda e: e.tensor_scalar(
                        tw0[:, 0:256], tw0[:, 0:256], -0.5, None, OP.is_lt),
                        reads=("tw0",), writes=("tw0",))
                    s.emit("pool", lambda e: e.tensor_tensor(
                        tw1[:, 0:256], tw1[:, 0:256], tw0[:, 0:256], OP.subtract),
                        reads=("tw0", "tw1"), writes=("tw1",))
                    s.emit("pool", lambda e, sl=sl: e.tensor_scalar(
                        wef_bf[:, sl], tw1[:, 0:256], sred[:, 1:2], None, OP.mult),
                        reads=("tw1", "sr1"), writes=("wef",))
                for ic in range(8):
                    s.emit("pe", lambda e, ic=ic: e.transpose(
                        psT3[:], wef_bf[:, ic * 128:(ic + 1) * 128], eye_bf_s[:]),
                        reads=("wef", "eyeb"), writes=("psT3",))
                    s.emit("dve", lambda e, ic=ic, rt=rt: e.tensor_copy(
                        weffT[:, ic * 4096 + rt * 128: ic * 4096 + (rt + 1) * 128], psT3[:]),
                        reads=("psT3",), writes=("weffT",))
            for rt in range(8):
                qb_ = ((fstg0, "fstg0"), (tw0, "tw0"), (fstg1, "fstg1"), (tw1, "tw1"))
                for qi, (buf, kb) in enumerate(qb_):
                    s.emit("sp", lambda e, rt=rt, qi=qi, buf=buf: e.dma_start(
                        out=buf[:], in_=wproj[rt * 128:(rt + 1) * 128,
                                              qi * 1024:(qi + 1) * 1024]),
                        writes=(kb,), inc=16)
                    s.emit("dve", lambda e, buf=buf, qi=qi: e.reduce_sum(
                        sred[:, qi:qi + 1], buf[:], axis=AXX, apply_absolute_value=True),
                        reads=(kb,), writes=(f"sq{qi}",))
                s.emit("dve", lambda e: e.tensor_tensor(
                    sred[:, 0:1], sred[:, 0:1], sred[:, 1:2], OP.add),
                    reads=("sq0", "sq1"), writes=("sq0",))
                s.emit("dve", lambda e: e.tensor_tensor(
                    sred[:, 2:3], sred[:, 2:3], sred[:, 3:4], OP.add),
                    reads=("sq2", "sq3"), writes=("sq2",))
                s.emit("dve", lambda e: e.tensor_tensor(
                    sred[:, 0:1], sred[:, 0:1], sred[:, 2:3], OP.add),
                    reads=("sq0", "sq2"), writes=("sq0",))
                s.emit("dve", lambda e: e.tensor_scalar(
                    sred[:, 1:2], sred[:, 0:1], 1.0 / 4096, 1e-5, OP.mult, OP.max),
                    reads=("sq0",), writes=("sr1",))
                s.emit("dve", lambda e: e.reciprocal(sred[:, 2:3], sred[:, 1:2]),
                       reads=("sr1",), writes=("sr2",))
                for qi, (buf, kb) in enumerate(qb_):
                    s.emit("pool", lambda e, buf=buf: e.tensor_scalar(
                        lcx[:], buf[:], sred[:, 2:3], None, OP.mult),
                        reads=(kb, "sr2"), writes=("lcxw",))
                    s.emit("pool", lambda e: e.tensor_scalar(
                        lsq[:], lcx[:], 0.5, None, OP.is_gt),
                        reads=("lcxw",), writes=("lsqw",))
                    s.emit("pool", lambda e: e.tensor_scalar(
                        lcx[:], lcx[:], -0.5, None, OP.is_lt),
                        reads=("lcxw",), writes=("lcxw",))
                    s.emit("pool", lambda e: e.tensor_tensor(
                        lsq[:], lsq[:], lcx[:], OP.subtract),
                        reads=("lcxw", "lsqw"), writes=("lsqw",))
                    s.emit("pool", lambda e: e.tensor_scalar(
                        wef_bf[:], lsq[:], sred[:, 1:2], None, OP.mult),
                        reads=("lsqw", "sr1"), writes=("wef",))
                    for icq in range(8):
                        icc = qi * 8 + icq
                        s.emit("pe", lambda e, icq=icq: e.transpose(
                            psT3[:], wef_bf[:, icq * 128:(icq + 1) * 128], eye_bf_s[:]),
                            reads=("wef", "eyeb"), writes=("psT3",))
                        s.emit("dve", lambda e, icc=icc, rt=rt: e.tensor_copy(
                            wprojT[:, icc * 1024 + rt * 128: icc * 1024 + (rt + 1) * 128],
                            psT3[:]),
                            reads=("psT3",), writes=("wprojT",))
            s.emit("sp", lambda e: e.dma_start(out=fbT_s[:], in_=fbT[:]), writes=("fbt",), inc=16)
            s.emit("sp", lambda e: e.dma_start(out=gt_s[:], in_=gate[:]), writes=("gtk",), inc=16)
            s.emit("act", lambda e: e.activation(out=fbT_bf[:], in_=fbT_s[:], func=AF.Copy),
                   reads=("fbt",), writes=("fbtb",))
            s.emit("sp", lambda e: e.dma_start(out=fstg0[0:16, :], in_=vmat[:]),
                   writes=("fstg0",), inc=16)
            s.emit("act", lambda e: e.activation(out=vm_bf[:], in_=fstg0[0:16, :], func=AF.Copy),
                   reads=("fstg0",), writes=("vmb",))
            for qq in range(4):
                s.emit("sp", lambda e, qq=qq: e.dma_start(
                    out=fstg0[0:16, :], in_=uT[:, qq * 1024:(qq + 1) * 1024]),
                    writes=("fstg0",), inc=16)
                s.emit("act", lambda e, qq=qq: e.activation(
                    out=uT_bf[:, qq * 1024:(qq + 1) * 1024], in_=fstg0[0:16, :], func=AF.Copy),
                    reads=("fstg0",), writes=("utb",))
            for half in range(2):
                s.emit("pe", lambda e, half=half: e.matmul(
                    psA[:], fbT_bf[:], vm_bf[:, half * 512:(half + 1) * 512],
                    start=True, stop=True, skip_group_check=True),
                    reads=("fbtb", "vmb"), writes=("psA",))
                s.emit("dve", lambda e, half=half: e.tensor_copy(
                    A_bf[:, half * 512:(half + 1) * 512], psA[:]),
                    reads=("psA",), writes=("Abf",))
            s.emit("pe", lambda e: e.matmul(psM[:, 0:1], ones_f32_s[:], gt_s[:],
                                            start=True, stop=True, skip_group_check=True),
                   reads=("onesf", "gtk"), writes=("psM",))
            s.emit("dve", lambda e: e.tensor_copy(gt_col[:], psM[:, 0:1]),
                   reads=("psM",), writes=("gtc",))
            for ic in range(8):
                for og in range(8):
                    s.emit("pe", lambda e, ic=ic, og=og: e.matmul(
                        psM[:], A_bf[:, ic * 128:(ic + 1) * 128],
                        uT_bf[:, og * 512:(og + 1) * 512],
                        start=True, stop=True, skip_group_check=True),
                        reads=("Abf", "utb", "gtc"), writes=("psM",))
                    s.emit("dve", lambda e, ic=ic, og=og: e.scalar_tensor_tensor(
                        weffT[:, ic * 4096 + og * 512: ic * 4096 + (og + 1) * 512],
                        psM[:], gt_col[:],
                        weffT[:, ic * 4096 + og * 512: ic * 4096 + (og + 1) * 512],
                        OP.mult, OP.add),
                        reads=("psM", "gtc", "weffT"), writes=("weffT",))
            s.fence()
            for st in range(NS):
                s.emit("sp", lambda e, st=st: e.dma_start(
                    out=x3f[:], in_=x3_dram[st * 128:(st + 1) * 128, :]),
                    writes=("x3f",), inc=16)
                s.emit("act", lambda e: e.activation(out=x3bf[:], in_=x3f[:], func=AF.Copy),
                       reads=("x3f",), writes=("x3bf",))
                for ic in range(8):
                    s.emit("pe", lambda e, ic=ic: e.transpose(
                        psT3[:], x3bf[:, ic * 128:(ic + 1) * 128], eye_bf_s[:]),
                        reads=("x3bf", "eyeb"), writes=("psT3",))
                    s.emit("dve", lambda e, ic=ic: e.tensor_copy(
                        x3T_st[:, ic * 128:(ic + 1) * 128], psT3[:]),
                        reads=("psT3",), writes=("x3T",))
                for og in range(8):
                    for ic in range(8):
                        s.emit("pe", lambda e, og=og, ic=ic: e.matmul(
                            psH[:], x3T_st[:, ic * 128:(ic + 1) * 128],
                            weffT[:, ic * 4096 + og * 512: ic * 4096 + (og + 1) * 512],
                            start=(ic == 0), stop=(ic == 7), skip_group_check=True),
                            reads=("x3T", "weffT"), writes=("psH",))
                    s.emit("act", lambda e: e.activation(out=rl_sb[:], in_=psH[:], func=AF.Relu),
                           reads=("psH",), writes=("rlk",))
                    s.emit("act", lambda e, og=og: e.activation(
                        out=rr_bf[:, og * 512:(og + 1) * 512], in_=rl_sb[:], func=AF.Square),
                        reads=("rlk",), writes=("rrk",))
                for grp in range(8):
                    for j in range(4):
                        icc = grp * 4 + j
                        s.emit("pe", lambda e, icc=icc, j=j: e.transpose(
                            psT4[:, j * 128:(j + 1) * 128],
                            rr_bf[:, icc * 128:(icc + 1) * 128], eye_bf_s[:]),
                            reads=("rrk", "eyeb"), writes=("psT4",))
                    s.emit("dve", lambda e, grp=grp: e.tensor_copy(
                        rrT[:, grp * 512:(grp + 1) * 512], psT4[:]),
                        reads=("psT4",), writes=("rrTk",))
                for og2 in range(2):
                    for icc in range(32):
                        s.emit("pe", lambda e, icc=icc, og2=og2: e.matmul(
                            psO3[:], rrT[:, icc * 128:(icc + 1) * 128],
                            wprojT[:, icc * 1024 + og2 * 512: icc * 1024 + (og2 + 1) * 512],
                            start=(icc == 0), stop=(icc == 31), skip_group_check=True),
                            reads=("rrTk", "wprojT"), writes=("psO3",))
                    s.emit("dve", lambda e, og2=og2: e.tensor_tensor(
                        outt[:, og2 * 512:(og2 + 1) * 512], x3f[:, og2 * 512:(og2 + 1) * 512],
                        psO3[:], OP.add),
                        reads=("x3f", "psO3"), writes=("outtk",))
                emit_ln(s, ln_bufs, outt, "outtk", x3f[:], "x3f")
                s.emit("pool", lambda e, st=st: e.dma_start(
                    out=out_sh[st * 128:(st + 1) * 128, :], in_=x3f[:]),
                    reads=("x3f",), inc=16)
        s.finish()
        s.run(nc)
    return nc



_CACHE = {}


def _get(name, builder):
    if name not in _CACHE:
        _CACHE[name] = builder()
    return _CACHE[name]


def rope_tables():
    inv = (1.0 / 10000.0 ** (np.arange(0, HD, 2, dtype=np.float64) / HD))
    pos = np.arange(NSEQ, dtype=np.float64)
    ang = pos[:, None] * inv[None, :]
    c32 = np.cos(ang)
    s32 = np.sin(ang)
    cosd = np.empty((NSEQ, 128), dtype=np.float32)
    sind = np.empty((NSEQ, 128), dtype=np.float32)
    for h in range(2):
        cosd[:, h * 64:h * 64 + 32] = c32
        cosd[:, h * 64 + 32:h * 64 + 64] = c32
        sind[:, h * 64:h * 64 + 32] = -s32
        sind[:, h * 64 + 32:h * 64 + 64] = s32
    return cosd.astype(ml_dtypes.bfloat16), sind.astype(ml_dtypes.bfloat16)


def kernel(L0_attn_x, L0_attn_qkv_w, L0_attn_qkv_b, L0_attn_proj_w, L0_attn_proj_b,
           L0_ffn_gate_w, L0_ffn_gate_bias, L0_ffn_w_gate, L0_ffn_w_up, L0_ffn_w_down,
           L0_ffn_sh_gate, L0_ffn_sh_up, L0_ffn_sh_down,
           L1_attn_Wq, L1_attn_bq, L1_attn_Wk, L1_attn_bk, L1_attn_Wv, L1_attn_bv,
           L1_ffn_fast_b, L1_ffn_w_fc, L1_ffn_w_proj, L1_ffn_u, L1_ffn_v, L1_ffn_gate):
    x = _f32(L0_attn_x).reshape(T, D)
    qkv_w = _f32(L0_attn_qkv_w)
    qkv_b = _f32(L0_attn_qkv_b)

    eye_bf = np.eye(128, dtype=ml_dtypes.bfloat16)
    eye_f32 = np.eye(128, dtype=np.float32)
    ones_bf = np.ones((1, 128), dtype=ml_dtypes.bfloat16)
    ones_f32 = np.ones((1, 128), dtype=np.float32)

    cosd, sind = rope_tables()

    nc1 = _get("l1", build_launch1)
    in1 = []
    for c in range(NCORES):
        r = 128 * c
        in1.append(dict(
            x=x,
            qwT=np.ascontiguousarray(qkv_w[r:r + 128, :].T),
            kwT=np.ascontiguousarray(qkv_w[1024 + r:1024 + r + 128, :].T),
            vwT=np.ascontiguousarray(qkv_w[2048 + r:2048 + r + 128, :].T),
            qb_row=np.ascontiguousarray(qkv_b[r:r + 128]).reshape(1, 128),
            kb_row=np.ascontiguousarray(qkv_b[1024 + r:1024 + r + 128]).reshape(1, 128),
            vb_row=np.ascontiguousarray(qkv_b[2048 + r:2048 + r + 128]).reshape(1, 128),
            cosd=cosd, sind=sind,
            eye_bf=eye_bf, eye_f32=eye_f32, ones_bf=ones_bf,
        ))
    res1 = run_bass_kernel_spmd(nc1, in1, core_ids=list(range(NCORES)))
    p_shards = [np.ascontiguousarray(res1.results[c]["p_sh"]).reshape(TC, D)
                for c in range(NCORES)]

    nc2 = _get("l2", build_launch2)
    wgT = np.ascontiguousarray(_f32(L0_ffn_w_gate).transpose(0, 2, 1))
    wuT = np.ascontiguousarray(_f32(L0_ffn_w_up).transpose(0, 2, 1))
    wdT = np.ascontiguousarray(_f32(L0_ffn_w_down).transpose(0, 2, 1))
    shgT = np.ascontiguousarray(_f32(L0_ffn_sh_gate).T)
    shuT = np.ascontiguousarray(_f32(L0_ffn_sh_up).T)
    shdT = np.ascontiguousarray(_f32(L0_ffn_sh_down).T)
    projT = np.ascontiguousarray(_f32(L0_attn_proj_w).T)
    projb = _f32(L0_attn_proj_b).reshape(1, D)
    gatewT = np.ascontiguousarray(_f32(L0_ffn_gate_w).T)
    gateb = _f32(L0_ffn_gate_bias).reshape(1, 32)
    in2 = []
    for c in range(NCORES):
        in2.append(dict(
            p_sh=p_shards[c],
            x_own=np.ascontiguousarray(x[TC * c: TC * (c + 1)]),
            projT=projT, projb=projb, gatewT=gatewT, gateb=gateb,
            wgT=wgT, wuT=wuT, wdT=wdT, shgT=shgT, shuT=shuT, shdT=shdT,
            eye_bf=eye_bf, eye_f32=eye_f32, ones_bf=ones_bf, ones_f32=ones_f32,
        ))
    res2 = run_bass_kernel_spmd(nc2, in2, core_ids=list(range(NCORES)))
    x2 = np.concatenate([res2.results[c]["x2_sh"] for c in range(NCORES)], axis=0)

    nc3 = _get("l3", build_launch3)
    wq_ = _f32(L1_attn_Wq)
    wk_ = _f32(L1_attn_Wk)
    wv_ = _f32(L1_attn_Wv)
    bq_c = np.ascontiguousarray(_f32(L1_attn_bq).reshape(8, 128).T)
    bk_c = np.ascontiguousarray(_f32(L1_attn_bk).reshape(8, 128).T)
    bv_row = _f32(L1_attn_bv).reshape(1, D)
    fbT = np.ascontiguousarray(_f32(L1_ffn_fast_b).T)
    vmat = _f32(L1_ffn_v)
    uTh = np.ascontiguousarray(_f32(L1_ffn_u).T)
    gate = _f32(L1_ffn_gate).reshape(1, 1)
    wfc = _f32(L1_ffn_w_fc)
    wproj = _f32(L1_ffn_w_proj)
    in3 = []
    for c in range(NCORES):
        own = x2[TC * c: TC * (c + 1)]
        partner = x2[TC * (c ^ 1): TC * ((c ^ 1) + 1)]
        in3.append(dict(
            xkv=np.ascontiguousarray(np.concatenate([own, partner], axis=0)),
            wq=wq_, wk=wk_, wv=wv_, bq_c=bq_c, bk_c=bk_c, bv_row=bv_row,
            fbT=fbT, vmat=vmat, uT=uTh, gate=gate, wfc=wfc, wproj=wproj,
            eye_bf=eye_bf, eye_f32=eye_f32, ones_bf=ones_bf, ones_f32=ones_f32,
        ))
    res3 = run_bass_kernel_spmd(nc3, in3, core_ids=list(range(NCORES)))
    out = np.concatenate([res3.results[c]["out_sh"] for c in range(NCORES)], axis=0)
    return np.ascontiguousarray(out.reshape(B, NSEQ, D)).astype(np.float32)
